# version 06
# baseline (speedup 1.0000x reference)
import numpy as np
from contextlib import ExitStack

import concourse.bass as bass
import concourse.mybir as mybir
from concourse.bass_utils import run_bass_kernel_spmd

F32 = mybir.dt.float32
BF16 = mybir.dt.bfloat16
FP8 = mybir.dt.float8e4
A = mybir.AluOpType
AF = mybir.ActivationFunctionType
AX = mybir.AxisListType.X

class _StopBuild(Exception):
    pass


_LIMIT = None
_STOPPED = [False]


def _ckpt(name):
    if _LIMIT is not None and name == _LIMIT:
        _STOPPED[0] = True
    if _STOPPED[0]:
        raise _StopBuild


T = 2048
D = 1024
NP = 8
NC = 16
I = 4096
MAGIC = float(3 * 2 ** 22)
ROW = 4096
VG = 68


DMA_GROUPS = {
    "dst": "sy", "dwl0": "sy", "dwl1": "sy", "dxl": "sy", "dfm": "sy", "dwt": "sy",
    "dcf": "sy", "dqf": "sy", "dgw": "sy", "ddw": "sy", "dhr": "sy", "dxo": "sy",
    "dww": "sc", "dxw": "sc", "dcw": "sc", "dhw": "sc", "dou": "sc",
}


class Bld:
    def __init__(self, nc, st):
        self.nc = nc
        names = ["vch", "sch", "gch", "tch"] + list(DMA_GROUPS)
        self.sem = {k: st.enter_context(nc.semaphore(k)) for k in names}
        self.cnt = {k: 0 for k in names}
        self.bank_free = [("vch", 0)] * 8

    def inc(self, key, inst, n=1):
        inst.then_inc(self.sem[key], n)
        self.cnt[key] += n
        return (key, self.cnt[key])

    def ev(self, key):
        return (key, self.cnt[key])

    def wait(self, eng, *evs):
        for key, val in evs:
            if val > 0:
                eng.wait_ge(self.sem[key], val)

    def v(self, inst):
        return self.inc("vch", inst)

    def s(self, inst):
        return self.inc("sch", inst)

    def g(self, inst):
        return self.inc("gch", inst)

    def t(self, inst):
        return self.inc("tch", inst)

    def dma(self, q, out, in_, transpose=False):
        eng = self.nc.sync if DMA_GROUPS[q] == "sy" else self.nc.scalar
        return self.inc(q, eng.dma_start(out=out, in_=in_, transpose=transpose), 16)

    def vsync(self):
        self.wait(self.nc.vector, self.ev("vch"))

    def ssync(self):
        self.wait(self.nc.scalar, self.ev("sch"))


def build_program():
    nc = bass.Bass("TRN2", target_bir_lowering=False)
    V, S, G, TE, SY = nc.vector, nc.scalar, nc.gpsimd, nc.tensor, nc.sync

    def din(name, shape):
        return nc.dram_tensor(name, shape, F32, kind="ExternalInput").ap()

    xo_d = din("xo", [T, D])
    xr_d = din("xr", [T, D])
    wq_d, wk_d, wv_d, wo_d = (din(n, [D, D]) for n in ("wq", "wk", "wv", "wo"))
    bq_d, bk_d, bv_d, bo_d = (din(n, [1, D]) for n in ("bq", "bk", "bv", "bo"))
    gw_d, uw_d = din("gw", [I, D]), din("uw", [I, D])
    dw_d = din("dw", [D, I])
    sin_d, sind_d = din("sin", [1, 1]), din("sind", [1, 1])
    out_d = nc.dram_tensor("out", [T, D], F32, kind="ExternalOutput").ap()

    def scr(name, shape):
        return nc.dram_tensor(name, shape, BF16, kind="Internal").ap()

    sc_xn = scr("sc_xn", [T, D])
    sc_xnq = scr("sc_xnq", [T, D])
    sc_xnr = scr("sc_xnr", [T, D])
    sc_wg = scr("sc_wg", [I, D])
    sc_wu = scr("sc_wu", [I, D])
    sc_wd = scr("sc_wd", [D, I])
    sc_w4 = scr("sc_w4", [4, D, D])
    sc_ctx = scr("sc_ctx", [T, D])
    sc_hq = scr("sc_hq", [T, I])

    try:
      with ExitStack() as st0:
        b = Bld(nc, st0)

        def sb(stk, name, shape, dt=F32):
            return stk.enter_context(nc.sbuf_tensor(name, shape, dt))

        pb = [st0.enter_context(nc.psum_tensor(f"pb{i}", [128, 512], F32))
              for i in range(7)]
        pb7x = st0.enter_context(nc.psum_tensor("pb7x", [128, 512], F32))
        pbb = pb7x[:, 64:65]

        identb = sb(st0, "identb", [128, 128], BF16)
        identf = sb(st0, "identf", [128, 128], F32)
        ones_r = sb(st0, "ones_r", [1, T], BF16)
        onesf = sb(st0, "onesf", [1, 128], F32)
        brow = sb(st0, "brow", [1, 4 * D], BF16)
        sc_t = sb(st0, "sc_t", [1, 2], F32)
        qsc = sb(st0, "qsc", [128, 15], F32)
        gs_w = sb(st0, "gs_w", [1, 3], F32)
        gmax = sb(st0, "gmax", [128, 3], F32)

        st0.enter_context(nc.Block())

        def qscal(i):
            return (qsc[:, 3 * i:3 * i + 1], qsc[:, 3 * i + 1:3 * i + 2],
                    qsc[:, 3 * i + 2:3 * i + 3])

        _qn = [0]

        def make_quant(stk):
            _qn[0] += 1
            pre = f"qw{_qn[0]}_"
            w = {k: sb(stk, pre + k, [128, 2048]) for k in ("cw", "ua", "ub", "uc")}
            w.update({k: sb(stk, pre + k, [128, 2048], BF16) for k in ("Rb", "v1", "v2")})
            w.update({k: sb(stk, pre + k, [128, 128]) for k in ("am", "s8f", "rh", "dq")})
            w["s8"] = sb(stk, pre + "s8", [128, 128], FP8)
            return w

        def quant(qw, x_ap, y_ap, ncols, scal, amax_ap=None):
            gs12, gsq, invgs2 = scal
            nb = ncols // 16
            e = None
            if amax_ap is None:
                x3 = x_ap.rearrange("p (nb bl) -> p nb bl", bl=16)
                b.v(V.tensor_reduce(qw["am"][:, 0:nb], x3, AX, A.max,
                                    apply_absolute_value=True))
                b.vsync()
                amax_ap = qw["am"][:, 0:nb]
            b.v(V.tensor_scalar(qw["s8"][:, 0:nb], amax_ap, gs12, 224.0, A.mult, A.min))
            b.vsync()
            b.v(V.tensor_copy(qw["s8f"][:, 0:nb], qw["s8"][:, 0:nb]))
            b.vsync()
            b.v(V.tensor_scalar(qw["rh"][:, 0:nb], qw["s8f"][:, 0:nb], 1e-30, None, A.max))
            b.vsync()
            b.v(V.reciprocal(qw["rh"][:, 0:nb], qw["rh"][:, 0:nb]))
            b.vsync()
            b.v(V.tensor_scalar(qw["rh"][:, 0:nb], qw["rh"][:, 0:nb], gsq, None, A.mult))
            b.v(V.tensor_scalar(qw["dq"][:, 0:nb], qw["s8f"][:, 0:nb], invgs2, None, A.mult))
            b.vsync()
            x3 = x_ap.rearrange("p (nb bl) -> p nb bl", bl=16)
            r3 = qw["rh"][:, 0:nb].unsqueeze(2).broadcast_to([128, nb, 16])
            c3 = qw["cw"][:, 0:ncols].rearrange("p (nb bl) -> p nb bl", bl=16)
            b.v(V.tensor_tensor(c3, x3, r3, A.mult))
            b.vsync()
            cw, ua, ub, uc = (qw[k][:, 0:ncols] for k in ("cw", "ua", "ub", "uc"))
            Rb, v1, v2 = (qw[k][:, 0:ncols] for k in ("Rb", "v1", "v2"))
            b.v(V.tensor_scalar(uc, cw, 3.0, -3.0, A.min, A.max))
            b.v(V.tensor_scalar(ua, cw, 2.0, -2.0, A.min, A.max))
            b.v(V.tensor_scalar(ub, cw, 1.0, -1.0, A.min, A.max))
            b.vsync()
            b.v(V.scalar_tensor_tensor(cw, ub, 2.0, ua, A.mult, A.add))
            b.vsync()
            b.v(V.tensor_tensor(cw, uc, cw, A.add))
            b.vsync()
            b.v(V.tensor_scalar(Rb, cw, MAGIC, MAGIC, A.add, A.subtract))
            b.vsync()
            b.v(V.tensor_scalar(v1, Rb, 4.0, -4.0, A.min, A.max))
            b.v(V.tensor_scalar(v2, Rb, 6.0, -6.0, A.min, A.max))
            b.vsync()
            b.v(V.scalar_tensor_tensor(v2, v1, -0.5, v2, A.mult, A.subtract))
            b.vsync()
            b.v(V.scalar_tensor_tensor(v1, Rb, 2.0, v2, A.mult, A.add))
            b.vsync()
            q3 = v1.rearrange("p (nb bl) -> p nb bl", bl=16)
            d3 = qw["dq"][:, 0:nb].unsqueeze(2).broadcast_to([128, nb, 16])
            y3 = y_ap.rearrange("p (nb bl) -> p nb bl", bl=16)
            e = b.v(V.tensor_tensor(y3, q3, d3, A.mult))
            b.vsync()
            return e

        b.dma("dst", sc_t[0:1, 0:1], sin_d)
        b.dma("dst", sc_t[0:1, 1:2], sind_d)
        with ExitStack() as stp:
            io_j = sb(stp, "io_j", [128, 128], F32)
            io_p = sb(stp, "io_p", [128, 128], F32)
            brow_f = sb(stp, "brow_f", [1, 4 * D], F32)
            for i, bd in enumerate((bq_d, bk_d, bv_d, bo_d)):
                e_ldb = b.dma("dst", brow_f[0:1, i * D:(i + 1) * D], bd)
            e_ld0 = e_ldb
            b.g(G.iota(io_j[:, :], [[1, 128]], channel_multiplier=0,
                       allow_small_or_imprecise_dtypes=True))
            e_io = b.g(G.iota(io_p[:, :], [[0, 128]], channel_multiplier=1,
                              allow_small_or_imprecise_dtypes=True))
            b.wait(V, e_io)
            b.v(V.tensor_tensor(identb[:, :], io_j[:, :], io_p[:, :], A.is_equal))
            b.v(V.tensor_tensor(identf[:, :], io_j[:, :], io_p[:, :], A.is_equal))
            b.v(V.memset(ones_r[:, :], 1.0))
            b.v(V.memset(onesf[:, :], 1.0))
            b.wait(V, e_ldb)
            b.v(V.tensor_copy(brow[0:1, :], brow_f[0:1, :]))
            b.vsync()
            e_setup = b.ev("vch")

            def bcast_scalar(src_ap, dst_col, ev_pre):
                b.wait(TE, ev_pre, e_setup, b.bank_free[7] if False else b.ev("vch"))
                eg = b.t(TE.matmul(pbb, onesf[:, :], src_ap,
                                   start=True, stop=True))
                b.wait(V, eg)
                b.v(V.tensor_copy(qsc[:, dst_col:dst_col + 1], pbb))
                b.vsync()

            def derive_scal(i):
                g0 = 3 * i
                b.v(V.tensor_scalar(qsc[:, g0 + 1:g0 + 2], qsc[:, g0:g0 + 1], 0.25,
                                    None, A.mult))
                b.v(V.reciprocal(qsc[:, g0 + 2:g0 + 3], qsc[:, g0:g0 + 1]))
                b.vsync()
                b.v(V.tensor_scalar(qsc[:, g0:g0 + 1], qsc[:, g0:g0 + 1], 1.0 / 12.0,
                                    None, A.mult))
                b.v(V.tensor_scalar(qsc[:, g0 + 2:g0 + 3], qsc[:, g0 + 2:g0 + 3], 2.0,
                                    None, A.mult))
                b.vsync()

            _ckpt("p0")
            bcast_scalar(sc_t[0:1, 0:1], 0, e_ld0)
            bcast_scalar(sc_t[0:1, 1:2], 3, e_ld0)
            derive_scal(0)
            derive_scal(1)

            with ExitStack() as stw:
                wbuf = [sb(stw, f"wbuf{i}", [128, 4096]) for i in range(2)]
                wqb = sb(stw, "wqb", [128, 4096], BF16)
                amax_w = sb(stw, "amax_w", [128, 3 * 2048], F32)
                qw = make_quant(stw)
                wspec = [
                    (gw_d, sc_wg, 32, 1024, 2, 0, 2),
                    (uw_d, sc_wu, 32, 1024, 2, 2048, 3),
                    (dw_d, sc_wd, 8, 4096, 1, 4096, 4),
                ]
                e_wu = [b.ev("vch")] * 2
                for (wd, _, nt, cols, pk, acb, _si) in wspec:
                    nb2 = (cols * pk) // 16
                    for ti in range(0, nt, pk):
                        bi = (ti // pk) % 2
                        buf = wbuf[bi]
                        b.wait(SY, e_wu[bi])
                        for j in range(pk):
                            e_l = b.dma(f"dwl{bi}", buf[:, j * cols:(j + 1) * cols],
                                        wd[(ti + j) * 128:(ti + j + 1) * 128, :])
                        b.wait(V, e_l)
                        x3 = buf[:, 0:cols * pk].rearrange("p (nb bl) -> p nb bl", bl=16)
                        col0 = acb + ti * (cols // 16)
                        e_wu[bi] = b.v(V.tensor_reduce(
                            amax_w[:, col0:col0 + nb2], x3, AX, A.max,
                            apply_absolute_value=True))
                b.vsync()
                for wi in range(3):
                    b.v(V.tensor_reduce(gmax[:, wi:wi + 1],
                                        amax_w[:, wi * 2048:(wi + 1) * 2048], AX, A.max))
                e_gm = b.ev("vch")
                b.wait(TE, e_gm)
                for wi in range(3):
                    e_tm = b.t(TE.transpose(pb[6][0:1, wi * 128:(wi + 1) * 128],
                                            gmax[:, wi:wi + 1], identf[:, :]))
                b.wait(V, e_tm)
                for wi in range(3):
                    b.v(V.tensor_reduce(gs_w[0:1, wi:wi + 1],
                                        pb[6][0:1, wi * 128:(wi + 1) * 128],
                                        AX, A.max))
                b.vsync()
                b.v(V.reciprocal(gs_w[0:1, 0:3], gs_w[0:1, 0:3]))
                b.vsync()
                b.v(V.tensor_scalar(gs_w[0:1, 0:3], gs_w[0:1, 0:3], 2688.0, None, A.mult))
                b.vsync()
                e_gsw = b.ev("vch")
                for wi in range(3):
                    bcast_scalar(gs_w[0:1, wi:wi + 1], 6 + 3 * wi, e_gsw)
                    derive_scal(2 + wi)
                _ckpt("p1")
                e_wu = [b.ev("vch")] * 2
                e_ws = b.ev("dww")
                for (wd, wscr, nt, cols, pk, acb, si) in wspec:
                    for ti in range(0, nt, pk):
                        bi = (ti // pk) % 2
                        buf = wbuf[bi]
                        b.wait(SY, e_wu[bi])
                        for j in range(pk):
                            e_l = b.dma(f"dwl{bi}", buf[:, j * cols:(j + 1) * cols],
                                        wd[(ti + j) * 128:(ti + j + 1) * 128, :])
                        b.wait(V, e_l, e_ws)
                        col0 = acb + ti * (cols // 16)
                        for sub in range(0, cols * pk, 2048):
                            sw = min(2048, cols * pk - sub)
                            e_q = quant(qw, buf[:, sub:sub + sw], wqb[:, sub:sub + sw],
                                        sw, qscal(si),
                                        amax_ap=amax_w[:, col0 + sub // 16:
                                                       col0 + (sub + sw) // 16])
                        e_wu[bi] = e_q
                        b.wait(S, e_q)
                        for j in range(pk):
                            e_ws = b.dma("dww", wscr[(ti + j) * 128:(ti + j + 1) * 128, :],
                                         wqb[:, j * cols:(j + 1) * cols])
                for i, wd in enumerate((wq_d, wk_d, wv_d, wo_d)):
                    for ti in range(0, 8, 2):
                        bi = (ti // 2) % 2
                        buf = wbuf[bi]
                        b.wait(SY, e_wu[bi])
                        for j in range(2):
                            e_l = b.dma(f"dwl{bi}", buf[:, j * 1024:(j + 1) * 1024],
                                        wd[(ti + j) * 128:(ti + j + 1) * 128, :])
                        b.wait(V, e_l, e_ws)
                        e_q = b.v(V.tensor_copy(wqb[:, 0:2048], buf[:, 0:2048]))
                        b.vsync()
                        e_wu[bi] = e_q
                        b.wait(S, e_q)
                        for j in range(2):
                            e_ws = b.dma("dww",
                                         sc_w4[i, (ti + j) * 128:(ti + j + 1) * 128, :],
                                         wqb[:, j * 1024:(j + 1) * 1024])

                _ckpt("p2")
                xa = sb(stw, "xa", [128, D])
                xnf = sb(stw, "xnf", [128, D])
                xnb = sb(stw, "xnb", [128, D], BF16)
                st6 = sb(stw, "st6", [128, 12])
                mv = sb(stw, "mv", [128, 2])
                muv = sb(stw, "muv", [128, 2])

                def ln_tile(e_load):
                    b.wait(V, e_load)
                    b.v(V.bn_stats(st6[:, 0:6], xa[:, 0:512]))
                    b.v(V.bn_stats(st6[:, 6:12], xa[:, 512:1024]))
                    b.vsync()
                    b.v(V.bn_aggr(mv[:, :], st6[:, :]))
                    b.vsync()
                    b.v(V.tensor_copy(muv[:, 0:1], mv[:, 0:1]))
                    e_v = b.v(V.tensor_scalar(mv[:, 1:2], mv[:, 1:2], 1e-5, None, A.add))
                    b.wait(S, e_v)
                    e_s = b.s(S.activation(mv[:, 1:2], mv[:, 1:2], AF.Sqrt))
                    b.wait(V, e_s)
                    b.v(V.reciprocal(muv[:, 1:2], mv[:, 1:2]))
                    b.vsync()
                    b.v(V.tensor_scalar(xnf[:, :], xa[:, :], muv[:, 0:1], muv[:, 1:2],
                                        A.subtract, A.mult))
                    b.vsync()

                e_xa = b.ev("vch")
                for c in range(NC):
                    b.wait(SY, e_xa)
                    e_l = b.dma("dxl", xa[:, :], xo_d[c * 128:(c + 1) * 128, :])
                    ln_tile(e_l)
                    b.wait(V, e_ws)
                    b.v(V.tensor_copy(xnb[:, :], xnf[:, :]))
                    b.vsync()
                    b.wait(S, b.ev("vch"))
                    e1 = b.dma("dxw", sc_xn[c * 128:(c + 1) * 128, :], xnb[:, :])
                    b.wait(V, e1)
                    e_q = quant(qw, xnf[:, :], xnb[:, :], 1024, qscal(0))
                    e_xa = e_q
                    b.wait(S, e_q)
                    e_ws = b.dma("dxw", sc_xnq[c * 128:(c + 1) * 128, :], xnb[:, :])
                for c in range(NC):
                    b.wait(SY, e_xa)
                    e_l = b.dma("dxl", xa[:, :], xr_d[c * 128:(c + 1) * 128, :])
                    ln_tile(e_l)
                    b.wait(V, e_ws)
                    e_c = b.v(V.tensor_copy(xnb[:, :], xnf[:, :]))
                    b.vsync()
                    e_xa = e_c
                    b.wait(S, e_c)
                    e_ws = b.dma("dxw", sc_xnr[c * 128:(c + 1) * 128, :], xnb[:, :])

            _ckpt("p3")
            with ExitStack() as sta:
                xn_fm = sb(sta, "xn_fm", [128, 8 * T], BF16)
                xnr_fm = sb(sta, "xnr_fm", [128, 8 * T], BF16)
                ctx = sb(sta, "ctx", [128, NC * D], BF16)
                wT = sb(sta, "wT", [128, 3 * 1024], BF16)
                qa = [sb(sta, f"qa{i}", [65, T], BF16) for i in range(2)]
                krow = [sb(sta, f"krow{i}", [65, ROW], BF16) for i in range(2)]
                vrow = [sb(sta, f"vrow{i}", [128, 32 * VG], BF16) for i in range(2)]
                m_all = sb(sta, "m_all", [128, 34], F32)
                pT = sb(sta, "pT", [128, 512], BF16)
                pTrow = sb(sta, "pTrow", [128, 2048], BF16)
                recl = sb(sta, "recl", [128, 2], F32)
                rtmp = sb(sta, "rtmp", [128, 1], F32)
                pvt = sb(sta, "pvt", [128, 64], F32)

                b.wait(SY, b.ev("dxw"), b.ev("dww"))
                for c in range(8):
                    b.dma("dfm", xn_fm[:, c * T:(c + 1) * T],
                          sc_xn[:, c * 128:(c + 1) * 128], transpose=True)
                    e_fm = b.dma("dfm", xnr_fm[:, c * T:(c + 1) * T],
                                 sc_xnr[:, c * 128:(c + 1) * 128], transpose=True)
                b.wait(V, b.ev("dxw"), b.ev("dww"))
                for i in range(2):
                    b.v(V.memset(krow[i][64:65, :], 1.0))
                    b.v(V.memset(vrow[i][:, :], 0.0))
                b.vsync()
                for i in range(2):
                    v3 = vrow[i][:, :].rearrange("p (g w) -> p g w", w=VG)
                    b.v(V.memset(v3[:, :, 64:65], 1.0))
                b.vsync()
                e_ones = b.ev("vch")

                def rcol(j):
                    if j < 64:
                        return j
                    if j < 1984:
                        return 2176 + (j - 64)
                    return 2112 + (j - 1984)

                def spans(t0, t1, off):
                    out, j = [], t0
                    while j < t1:
                        c0 = off(j)
                        jj = j + 1
                        while jj < t1 and off(jj) == c0 + (jj - j):
                            jj += 1
                        out.append((j - t0, jj - j, c0))
                        j = jj
                    return out

                _ckpt("p4")
                e_wT_used = ("tch", 0)
                for p in range(NP):
                    _ckpt(f"pair{p}")
                    b.wait(SY, e_wT_used)
                    for w in range(3):
                        wd4 = (0, 1, 2)[w]
                        for kc in range(8):
                            e_wt = b.dma(
                                "dwt",
                                wT[:, w * 1024 + kc * 128:w * 1024 + (kc + 1) * 128],
                                sc_w4[wd4, p * 128:(p + 1) * 128,
                                      kc * 128:(kc + 1) * 128],
                                transpose=True)
                    b.wait(TE, e_wt, e_fm, e_ones, e_setup)
                    _ckpt("w0")

                    def proj_fm(w_idx, bias_row, src_fm, evac):
                        for t4 in range(4):
                            bi = t4 % 2
                            bank = pb[bi]
                            b.wait(TE, b.bank_free[bi])
                            for kc in range(8):
                                TE.matmul(
                                    bank[:, :],
                                    wT[:, w_idx * 1024 + kc * 128:
                                       w_idx * 1024 + (kc + 1) * 128],
                                    src_fm[:, kc * T + t4 * 512:kc * T + (t4 + 1) * 512],
                                    start=(kc == 0), stop=False)
                            e_mm = b.t(TE.matmul(
                                bank[:, :],
                                brow[0:1, bias_row * D + p * 128:bias_row * D + (p + 1) * 128],
                                ones_r[:, t4 * 512:(t4 + 1) * 512],
                                start=False, stop=True))
                            evac(t4, bank, e_mm, bi)

                    def evac_qk(dst, col_of, scale):
                        def f(t4, bank, e_mm, bi):
                            b.wait(V, e_mm)
                            c0 = col_of(t4 * 512)
                            for hh in range(2):
                                if scale is None:
                                    e_e = b.v(V.tensor_copy(
                                        dst[hh][0:64, c0:c0 + 512],
                                        bank[hh * 64:hh * 64 + 64, :]))
                                else:
                                    e_e = b.v(V.tensor_scalar(
                                        dst[hh][0:64, c0:c0 + 512],
                                        bank[hh * 64:hh * 64 + 64, :],
                                        scale, None, A.mult))
                            b.bank_free[bi] = e_e
                        return f

                    def evac_k_spans(off):
                        def f(t4, bank, e_mm, bi):
                            b.wait(V, e_mm)
                            for (rel, ln, c0) in spans(t4 * 512, (t4 + 1) * 512, off):
                                for hh in range(2):
                                    e_e = b.v(V.tensor_copy(
                                        krow[hh][0:64, c0:c0 + ln],
                                        bank[hh * 64:hh * 64 + 64, rel:rel + ln]))
                            b.bank_free[bi] = e_e
                        return f

                    proj_fm(0, 0, xn_fm, evac_qk(qa, lambda t: t, 0.125))
                    _ckpt("q0")
                    proj_fm(1, 1, xn_fm, evac_k_spans(lambda t: 64 + t))
                    proj_fm(1, 1, xnr_fm, evac_k_spans(rcol))
                    _ckpt("k0")

                    def proj_v(src_fm, off):
                        for tc in range(16):
                            bi = 2 + (tc % 2)
                            bank = pb[bi]
                            b.wait(TE, b.bank_free[bi])
                            for kc in range(8):
                                TE.matmul(
                                    bank[:, 0:128],
                                    src_fm[:, kc * T + tc * 128:kc * T + (tc + 1) * 128],
                                    wT[:, 2 * 1024 + kc * 128:2 * 1024 + (kc + 1) * 128],
                                    start=(kc == 0), stop=False)
                            e_mm = b.t(TE.matmul(
                                bank[:, 0:128], ones_r[:, tc * 128:(tc + 1) * 128],
                                brow[0:1, 2 * D + p * 128:2 * D + (p + 1) * 128], start=False, stop=True))
                            b.wait(V, e_mm)
                            for (rel, ln, rp0) in spans(tc * 128, (tc + 1) * 128, off):
                                j = 0
                                while j < ln:
                                    rp = rp0 + j
                                    g64, part = rp // 64, rp % 64
                                    gg, phalf = g64 // 2, (g64 % 2) * 64
                                    take = min(ln - j, 64 - part)
                                    for hh in range(2):
                                        e_e = b.v(V.tensor_copy(
                                            vrow[hh][phalf + part:phalf + part + take,
                                                     gg * VG:gg * VG + 64],
                                            bank[rel + j:rel + j + take,
                                                 hh * 64:hh * 64 + 64]))
                                    j += take
                            b.bank_free[bi] = e_e

                    proj_v(xn_fm, lambda t: 64 + t)
                    proj_v(xn_fm, lambda t: 64 + t) if False else None
                    proj_v(xnr_fm, rcol)
                    _ckpt("v0")

                    for hh in range(2):
                        h = 2 * p + hh
                        qah, krh, vrh = qa[hh], krow[hh], vrow[hh]
                        mc = 17 * hh
                        for t in range(16):
                            bi = t % 2
                            bank = pb[bi]
                            b.wait(TE, b.bank_free[bi])
                            TE.matmul(bank[:, 0:256], qah[0:64, t * 128:(t + 1) * 128],
                                      krh[0:64, t * 128:t * 128 + 256],
                                      start=True, stop=True)
                            TE.matmul(bank[:, 256:320], qah[0:64, t * 128:(t + 1) * 128],
                                      krh[0:64, 0:64], start=True, stop=True)
                            e_mm = b.t(TE.matmul(bank[:, 320:384],
                                                 qah[0:64, t * 128:(t + 1) * 128],
                                                 krh[0:64, 64:128],
                                                 start=True, stop=True))
                            b.wait(V, e_mm)
                            e_m = b.v(V.tensor_reduce(m_all[:, mc + t:mc + t + 1],
                                                      bank[:, 0:384], AX, A.max,
                                                      negate=True))
                            b.bank_free[bi] = e_m
                        for c8 in range(8):
                            bi = c8 % 2
                            bank = pb[bi]
                            b.wait(TE, b.bank_free[bi])
                            e_mm = b.t(TE.matmul(bank[0:64, :], qah[0:64, 0:64],
                                                 krh[0:64, c8 * 512:(c8 + 1) * 512],
                                                 start=True, stop=True))
                            b.wait(V, e_mm)
                            if c8 == 0:
                                e_m = b.v(V.tensor_reduce(m_all[0:64, mc + 16:mc + 17],
                                                          bank[0:64, :], AX, A.max,
                                                          negate=True))
                            else:
                                b.v(V.tensor_reduce(rtmp[0:64, 0:1], bank[0:64, :],
                                                    AX, A.max, negate=True))
                                b.vsync()
                                e_m = b.v(V.tensor_tensor(m_all[0:64, mc + 16:mc + 17],
                                                          m_all[0:64, mc + 16:mc + 17],
                                                          rtmp[0:64, 0:1], A.min))
                                b.vsync()
                            b.bank_free[bi] = e_m
                        b.vsync()
                        _ckpt(f"h{hh}a")
                        b.wait(TE, b.ev("vch"), b.bank_free[2], b.bank_free[3],
                               b.bank_free[4], b.bank_free[5], b.bank_free[6])
                        for t in range(16):
                            e_tm = b.t(TE.transpose(
                                pb[2 + t // 4][0:1, (t % 4) * 128:(t % 4 + 1) * 128],
                                m_all[:, mc + t:mc + t + 1], identf[:, :]))
                        e_tm = b.t(TE.transpose(pb[6][0:1, 0:128],
                                                m_all[:, mc + 16:mc + 17],
                                                identf[:, :]))
                        b.wait(V, e_tm)
                        for c4 in range(4):
                            b.v(V.tensor_copy(qah[64:65, c4 * 512:(c4 + 1) * 512],
                                              pb[2 + c4][0:1, 0:512]))
                        b.vsync()
                        e_q64 = b.v(V.tensor_copy(qah[64:65, 0:64], pb[6][0:1, 0:64]))
                        for bi in range(2, 7):
                            b.bank_free[bi] = e_q64
                        b.wait(TE, e_q64)
                        _ckpt(f"h{hh}b")
                        for t in range(16):
                            bi = 2 + (t % 2)
                            bank = pb[bi]
                            b.wait(TE, b.bank_free[bi])
                            TE.matmul(bank[:, 0:128], krh[0:65, t * 128:(t + 1) * 128],
                                      qah[0:65, t * 128:(t + 1) * 128],
                                      start=True, stop=True)
                            TE.matmul(bank[:, 128:256],
                                      krh[0:65, (t + 1) * 128:(t + 2) * 128],
                                      qah[0:65, t * 128:(t + 1) * 128],
                                      start=True, stop=True)
                            TE.matmul(bank[0:64, 256:384], krh[0:65, 0:64],
                                      qah[0:65, t * 128:(t + 1) * 128],
                                      start=True, stop=True)
                            e_mm = b.t(TE.matmul(bank[64:128, 256:384], krh[0:65, 64:128],
                                                 qah[0:65, t * 128:(t + 1) * 128],
                                                 start=True, stop=True))
                            if t == 0:
                                _ckpt(f"h{hh}s2")
                            b.wait(S, e_mm, b.ev("sch"))
                            e_x = b.s(S.activation(pT[:, 0:384], bank[:, 0:384], AF.Exp))
                            b.bank_free[bi] = e_x
                            if t == 0:
                                _ckpt(f"h{hh}s3")
                            bo_i = 4 + (t % 2)
                            bank_o = pb[bo_i]
                            b.wait(TE, e_x, b.bank_free[bo_i])
                            TE.matmul(bank_o[:, 0:VG], pT[64:128, 0:128],
                                      vrh[64:128, t * VG:t * VG + VG],
                                      start=True, stop=False)
                            TE.matmul(bank_o[0:64, 0:VG], pT[0:64, 0:64],
                                      vrh[0:64, t * VG:t * VG + VG],
                                      start=False, stop=False)
                            TE.matmul(bank_o[64:128, 0:VG], pT[64:128, 192:256],
                                      vrh[64:128, (t + 1) * VG:(t + 1) * VG + VG],
                                      start=False, stop=False)
                            TE.matmul(bank_o[:, 0:VG], pT[0:64, 128:256],
                                      vrh[0:64, (t + 1) * VG:(t + 1) * VG + VG],
                                      start=False, stop=True)
                            e_pv = b.t(TE.matmul(bank_o[:, 128:128 + VG],
                                                 pT[0:128, 256:384],
                                                 vrh[0:128, 0:VG],
                                                 start=True, stop=True))
                            if t == 0:
                                _ckpt(f"h{hh}s4")
                            b.wait(V, e_pv)
                            b.v(V.reciprocal(recl[:, 0:1], bank_o[:, 64:65]))
                            b.v(V.reciprocal(recl[:, 1:2], bank_o[:, 192:193]))
                            b.vsync()
                            b.v(V.tensor_scalar(pvt[:, :], bank_o[:, 0:64],
                                                recl[:, 0:1], None, A.mult))
                            b.vsync()
                            e_n = b.v(V.scalar_tensor_tensor(
                                ctx[:, t * D + h * 64:t * D + h * 64 + 64],
                                bank_o[:, 128:192], recl[:, 1:2], pvt[:, :],
                                A.mult, A.add))
                            b.vsync()
                            b.bank_free[bo_i] = e_n
                            _ckpt(f"h{hh}t{t}")
                        _ckpt(f"h{hh}c")
                        for r4 in range(4):
                            bi = 2 + (r4 % 2)
                            bank = pb[bi]
                            b.wait(TE, b.bank_free[bi])
                            e_mm = None
                            for cc in range(8):
                                c = r4 * 8 + cc
                                e_mm = b.t(TE.matmul(bank[:, cc * 64:(cc + 1) * 64],
                                                     krh[0:65, c * 128:(c + 1) * 128],
                                                     qah[0:65, 0:64],
                                                     start=True, stop=True))
                            b.wait(S, e_mm, b.ev("sch"))
                            e_x = b.s(S.activation(pTrow[:, r4 * 512:(r4 + 1) * 512],
                                                   bank[:, :], AF.Exp))
                            b.bank_free[bi] = e_x
                        b.wait(TE, b.ev("sch"), b.bank_free[6])
                        e_pv = None
                        for c in range(32):
                            e_pv = b.t(TE.matmul(pb[6][0:64, 0:VG],
                                                 pTrow[:, c * 64:(c + 1) * 64],
                                                 vrh[:, c * VG:c * VG + VG],
                                                 start=(c == 0), stop=(c == 31)))
                        b.wait(V, e_pv)
                        b.v(V.reciprocal(recl[0:64, 0:1], pb[6][0:64, 64:65]))
                        b.vsync()
                        e_n = b.v(V.tensor_scalar(ctx[0:64, h * 64:h * 64 + 64],
                                                  pb[6][0:64, 0:64], recl[0:64, 0:1],
                                                  None, A.mult))
                        b.vsync()
                        b.bank_free[6] = e_n
                    e_wT_used = b.ev("tch")

                _ckpt("p5")
                b.wait(S, b.ev("vch"))
                for c in range(NC):
                    e_cs = b.dma("dcw", sc_ctx[c * 128:(c + 1) * 128, :],
                                 ctx[:, c * D:(c + 1) * D])
                b.wait(SY, e_cs)

            with ExitStack() as sts:
                s_sb = sb(sts, "s_sb", [128, NC * D], BF16)
                with ExitStack() as sto:
                    ctx_fm = sb(sto, "ctx_fm", [128, 8 * T], BF16)
                    woT = sb(sto, "woT", [128, 8 * 1024], BF16)
                    b.wait(SY, b.ev("tch"), b.ev("vch"))
                    for c in range(8):
                        b.dma("dcf", ctx_fm[:, c * T:(c + 1) * T],
                              sc_ctx[:, c * 128:(c + 1) * 128], transpose=True)
                        e_cf = b.dma("dcf", woT[:, c * 1024:(c + 1) * 1024],
                                     sc_w4[3, :, c * 128:(c + 1) * 128], transpose=True)
                    b.wait(TE, e_cf)
                    for tc in range(NC):
                        for gg in range(2):
                            bi = gg
                            bank = pb[bi]
                            b.wait(TE, b.bank_free[bi])
                            for kc in range(8):
                                TE.matmul(
                                    bank[:, :],
                                    ctx_fm[:, kc * T + tc * 128:kc * T + (tc + 1) * 128],
                                    woT[:, kc * 1024 + gg * 512:kc * 1024 + (gg + 1) * 512],
                                    start=(kc == 0), stop=False)
                            e_mm = b.t(TE.matmul(bank[:, :],
                                                 ones_r[:, tc * 128:(tc + 1) * 128],
                                                 brow[0:1, 3 * D + gg * 512:3 * D + (gg + 1) * 512],
                                                 start=False, stop=True))
                            b.wait(V, e_mm)
                            e_e = b.v(V.tensor_copy(
                                s_sb[:, tc * D + gg * 512:tc * D + (gg + 1) * 512],
                                bank[:, :]))
                            b.bank_free[bi] = e_e

                _ckpt("p6")
                with ExitStack() as stf:
                    xnq_fm = sb(stf, "xnq_fm", [128, 8 * T], BF16)
                    gwT_f = sb(stf, "gwT_f", [128, 8 * 1024], BF16)
                    uwT_f = sb(stf, "uwT_f", [128, 8 * 1024], BF16)
                    silu_t = sb(stf, "silu_t", [128, 1024])
                    hmul = sb(stf, "hmul", [128, 1024])
                    hq_t = sb(stf, "hq_t", [128, 1024], BF16)
                    qw = make_quant(stf)
                    b.wait(SY, b.ev("vch"))
                    for c in range(8):
                        e_qf = b.dma("dqf", xnq_fm[:, c * T:(c + 1) * T],
                                     sc_xnq[:, c * 128:(c + 1) * 128], transpose=True)
                    e_hq = b.ev("dhw")
                    for half in range(2):
                        for q4 in range(2):
                            b.wait(SY, b.ev("tch"))
                            r0 = half * 2048 + q4 * 1024
                            for kc in range(8):
                                b.dma("dgw", gwT_f[:, kc * 1024:(kc + 1) * 1024],
                                      sc_wg[r0:r0 + 1024, kc * 128:(kc + 1) * 128],
                                      transpose=True)
                                e_wl = b.dma("dgw", uwT_f[:, kc * 1024:(kc + 1) * 1024],
                                             sc_wu[r0:r0 + 1024, kc * 128:(kc + 1) * 128],
                                             transpose=True)
                            b.wait(TE, e_wl, e_qf)
                            for tc in range(NC):
                                for bi in range(4):
                                    b.wait(TE, b.bank_free[bi])
                                e_mm = None
                                for kc in range(8):
                                    xs = xnq_fm[:, kc * T + tc * 128:kc * T + (tc + 1) * 128]
                                    for gg in range(2):
                                        TE.matmul(
                                            pb[gg][:, :], xs,
                                            gwT_f[:, kc * 1024 + gg * 512:
                                                  kc * 1024 + (gg + 1) * 512],
                                            start=(kc == 0), stop=(kc == 7))
                                    for gg in range(2):
                                        mm = TE.matmul(
                                            pb[2 + gg][:, :], xs,
                                            uwT_f[:, kc * 1024 + gg * 512:
                                                  kc * 1024 + (gg + 1) * 512],
                                            start=(kc == 0), stop=(kc == 7))
                                e_mm = b.t(mm)
                                b.wait(S, e_mm, b.ev("sch"))
                                for gg in range(2):
                                    e_si = b.s(S.activation(silu_t[:, gg * 512:(gg + 1) * 512],
                                                            pb[gg][:, :], AF.Silu))
                                b.wait(V, e_si)
                                for gg in range(2):
                                    e_h = b.v(V.tensor_tensor(hmul[:, gg * 512:(gg + 1) * 512],
                                                              silu_t[:, gg * 512:(gg + 1) * 512],
                                                              pb[2 + gg][:, :], A.mult))
                                for bi in range(4):
                                    b.bank_free[bi] = e_h
                                b.vsync()
                                b.wait(V, e_hq)
                                e_q = quant(qw, hmul[:, :], hq_t[:, :], 1024, qscal(1))
                                b.wait(S, e_q)
                                e_hq = b.dma(
                                    "dhw",
                                    sc_hq[tc * 128:(tc + 1) * 128, r0:r0 + 1024],
                                    hq_t[:, :])
                    b.wait(SY, e_hq)

                _ckpt("p7a")
                with ExitStack() as std_:
                    dwT = sb(std_, "dwT", [128, 32 * 1024], BF16)
                    hqT = sb(std_, "hqT", [128, 32 * 256], BF16)
                    g_t = sb(std_, "g_t", [128, 1024])
                    xa2 = sb(std_, "xa2", [128, D])
                    out_t = sb(std_, "out_t", [128, D])
                    for ic in range(32):
                        e_dw = b.dma("ddw", dwT[:, ic * 1024:(ic + 1) * 1024],
                                     sc_wd[:, ic * 128:(ic + 1) * 128], transpose=True)
                    e_hqT = ("tch", 0)
                    e_out = b.ev("dou")
                    e_xl = b.ev("dxo")
                    for tp in range(8):
                        b.wait(SY, e_hqT)
                        for ic in range(32):
                            e_hT = b.dma("dhr", hqT[:, ic * 256:(ic + 1) * 256],
                                         sc_hq[tp * 256:(tp + 1) * 256,
                                               ic * 128:(ic + 1) * 128], transpose=True)
                        b.wait(TE, e_hT, e_dw)
                        for bi in range(4):
                            b.wait(TE, b.bank_free[bi])
                        e_mm = None
                        for ic in range(32):
                            for tt in range(2):
                                hs = hqT[:, ic * 256 + tt * 128:ic * 256 + (tt + 1) * 128]
                                for half in range(2):
                                    mm = TE.matmul(
                                        pb[2 * tt + half][:, :], hs,
                                        dwT[:, ic * 1024 + half * 512:
                                            ic * 1024 + (half + 1) * 512],
                                        start=(ic == 0), stop=(ic == 31))
                        e_mm = b.t(mm)
                        e_hqT = e_mm
                        for tt in range(2):
                            tc = 2 * tp + tt
                            b.wait(S, e_mm, b.ev("sch"))
                            for half in range(2):
                                e_si = b.s(S.activation(g_t[:, half * 512:(half + 1) * 512],
                                                        pb[2 * tt + half][:, :], AF.Silu))
                            b.wait(SY, e_out)
                            e_xl = b.dma("dxo", xa2[:, :], xo_d[tc * 128:(tc + 1) * 128, :])
                            b.wait(V, e_si, e_xl)
                            b.v(V.tensor_tensor(g_t[:, :], s_sb[:, tc * D:(tc + 1) * D],
                                                g_t[:, :], A.mult))
                            b.vsync()
                            e_o = b.v(V.tensor_tensor(out_t[:, :], xa2[:, :], g_t[:, :],
                                                      A.add))
                            b.bank_free[2 * tt] = e_o
                            b.bank_free[2 * tt + 1] = e_o
                            b.wait(S, e_o)
                            e_out = b.dma("dou", out_d[tc * 128:(tc + 1) * 128, :],
                                          out_t[:, :])
                    b.wait(SY, b.ev("dou"))
    except _StopBuild:
        pass
    except AssertionError:
        if _LIMIT is None:
            raise
    return nc


_PROG = None


def _get_prog():
    global _PROG
    if _PROG is None:
        _PROG = build_program()
    return _PROG


def kernel(**inputs):
    B, N = 4, 4096
    x = np.ascontiguousarray(np.asarray(inputs["L0_attn_x"], dtype=np.float32))
    perm0 = np.arange(2048)
    perm1 = (np.arange(2048).reshape(32, 64)[::-1]).reshape(-1) + 2048
    base = {
        "wq": inputs["L0_attn_Wq"], "wk": inputs["L0_attn_Wk"],
        "wv": inputs["L0_attn_Wv"], "wo": inputs["L0_attn_Wo"],
        "gw": inputs["L0_ffn_gate_w"], "uw": inputs["L0_ffn_up_w"],
        "dw": inputs["L0_ffn_down_w"],
    }
    base = {k: np.ascontiguousarray(np.asarray(v, dtype=np.float32))
            for k, v in base.items()}
    for k, src in (("bq", "L0_attn_bq"), ("bk", "L0_attn_bk"),
                   ("bv", "L0_attn_bv"), ("bo", "L0_attn_bo")):
        base[k] = np.ascontiguousarray(
            np.asarray(inputs[src], dtype=np.float32).reshape(1, -1))
    base["sin"] = np.asarray(inputs["L0_ffn_s_in"], dtype=np.float32).reshape(1, 1)
    base["sind"] = np.asarray(inputs["L0_ffn_s_in_down"], dtype=np.float32).reshape(1, 1)

    in_maps = []
    for c in range(8):
        bb, hh = c // 2, c % 2
        own = perm1 if hh else perm0
        rem = perm0 if hh else perm1
        m = dict(base)
        m["xo"] = np.ascontiguousarray(x[bb][own])
        m["xr"] = np.ascontiguousarray(x[bb][rem])
        in_maps.append(m)

    nc = _get_prog()
    res = run_bass_kernel_spmd(nc, in_maps, core_ids=list(range(8)))
    out = np.empty((B, N, D), dtype=np.float32)
    for c in range(8):
        bb, hh = c // 2, c % 2
        own = perm1 if hh else perm0
        out[bb][own] = res.results[c]["out"]
    return out
